# version 02 (final)
# baseline (speedup 1.0000x reference)
import numpy as np
import ml_dtypes
import concourse.bass as bass
import concourse.mybir as mybir
from concourse.bass_utils import run_bass_kernel_spmd

F32 = mybir.dt.float32
BF16 = mybir.dt.bfloat16
AF = mybir.ActivationFunctionType
ALU = mybir.AluOpType
AXX = mybir.AxisListType.X

B, S, D = 4, 2048, 2048
HI, DI, H, HD = 16, 128, 16, 128
FH = 8192
TOPK = 256
NBIS = 21
SENT = -60000.0
SCL = float(HD) ** -0.5
EXPB = -5.0
FWS = 1.0 / float(np.sqrt(FH))
NQ = 1024
J = 2
WJ = [1024, 2048]
KTJ = [8, 16]


class Ctr:
    __slots__ = ("h", "n")
    def __init__(self, h):
        self.h = h
        self.n = 0
    def inc(self, k=1):
        self.n += k
        return self.n


def build_nc(dbg=False):
    nc = bass.Bass("TRN2", target_bir_lowering=False)

    dr = {}
    for name, shape in [
        ("xT", [D, S]), ("xqT", [D, NQ]), ("xq", [NQ, D]),
        ("causal", [128, 4096]), ("taufloor", [128, 8]),
        ("cstf", [128, 132]), ("csth", [128, 128]),
        ("wq_idx", [D, 2048]), ("wk_idx", [D, DI]), ("w_iw", [D, HI]),
        ("wq", [D, 2048]), ("wk", [D, HD]), ("wv", [D, HD]), ("wo", [2048, D]),
        ("w1t", [D, FH]), ("w3t", [D, FH]), ("w2t", [FH, D]),
    ]:
        dr[name] = nc.dram_tensor(name, shape, F32, kind="ExternalInput").ap()
    out_d = nc.dram_tensor("out", [NQ, D], F32, kind="ExternalOutput").ap()
    spill_d = nc.dram_tensor("xnspill", [NQ, D], F32, kind="Internal").ap()
    dbg_out = {}
    if dbg:
        for nm, shape in [("d_ktil", [128, 2048]), ("d_kfm", [128, 2048]),
                          ("d_vtm", [128, 2048]), ("d_qtil", [128, 16384]),
                          ("d_qfm", [128, 16384]), ("d_iw", [128, 128]),
                          ("d_Ibf", [128, 12288]), ("d_tau", [128, 48]),
                          ("d_mT", [128, 8192]), ("d_mThi", [128, 4096]), ("d_ofm", [128, 16384]),
                          ("d_xn", [NQ, D]), ("d_xnfm", [128, 16384]),
                          ("d_hid", [128, 32768])]:
            dbg_out[nm] = nc.dram_tensor(nm, shape, F32, kind="ExternalOutput").ap()

    R1 = nc.sbuf_tensor("R1", [128, 32768], BF16).__enter__().ap()
    R2 = nc.sbuf_tensor("R2", [128, 32768], BF16).__enter__().ap()
    R3 = nc.sbuf_tensor("R3", [128, 20480], BF16).__enter__().ap()
    R4 = nc.sbuf_tensor("R4", [128, 9216], BF16).__enter__().ap()
    R5 = nc.sbuf_tensor("R5", [128, 2560], BF16).__enter__().ap()
    R6 = nc.sbuf_tensor("R6", [128, 6144], BF16).__enter__().ap()
    R1f, R2f, R4f, R5f = (r.bitcast(F32) for r in (R1, R2, R4, R5))

    def cvv(Rr, off, shape):
        flat = Rr[:, off:off + int(np.prod(shape))]
        if len(shape) == 1:
            return flat
        return flat.rearrange("p (a b) -> p a b", a=shape[0], b=shape[1])

    xfk = cvv(R1, 0, [16, 2048])
    xfq = cvv(R1, 0, [16, 1024])
    qtil = cvv(R1, 16384, [16, 1024])
    xqf = cvv(R1f, 0, [8, 2048])
    hid = cvv(R1, 0, [64, 512])

    qfm = cvv(R2, 0, [16, 1024])
    ofm = cvv(R2, 16384, [16, 1024])
    htmp = cvv(R2, 0, [64, 128])
    w1b = [cvv(R2, 8192 + i * 2048, [16, 128]) for i in range(2)]
    w3b = [cvv(R2, 12288 + i * 2048, [16, 128]) for i in range(2)]
    w2b = [cvv(R2, 16384 + i * 2048, [2048]) for i in range(2)]
    xnr = [cvv(R2f, 10240 + i * 2048, [2048]) for i in range(2)]
    silub = [cvv(R2, 28672 + i * 512, [512]) for i in range(2)]

    Ibf = R3[:, 0:12288]
    mT_lo = cvv(R3, 12288, [16, 512])
    mT_hi = cvv(R3, 0, [8, 512])
    xnfm = cvv(R3, 0, [16, 1024])
    wob = [cvv(R3, 16384 + i * 2048, [2048]) for i in range(2)]

    wkidx_sb = cvv(R4, 0, [16, 128])
    wk_sb = cvv(R4, 2048, [16, 128])
    wv_sb = cvv(R4, 4096, [16, 128])
    wiw_sb = cvv(R4, 6144, [16, 16])
    wqb = [cvv(R4, i * 2048, [16, 128]) for i in range(2)]
    caub = cvv(R4, 0, [4, 1024])
    Iacc = cvv(R4f, 2048, [2048])
    relub = [cvv(R4, 8192 + i * 512, [512]) for i in range(2)]
    maskb = R4[:, 0:2048]
    pbuf = [cvv(R4, i * 512, [512]) for i in range(2)]
    sqtrash = R4[:, 2048:2560]

    identf = cvv(R5f, 0, [128])
    cmisc = cvv(R5f, 128, [4])
    iwsb = cvv(R5f, 136, [8, 16])
    taust = cvv(R5f, 264, [48])
    lns = cvv(R5f, 312, [32])
    bcast_sb = cvv(R5f, 344, [512])
    hbf = R5[:, 1712:1840]
    identb = R5[:, 1840:1968]
    rrow = R5[:, 1968:2480]
    EXPB_AP = cmisc[:, 0:1]
    EPS_AP = cmisc[:, 1:2]
    NEG1024_AP = cmisc[:, 3:4]
    ones_col = hbf[:, 0:1]
    ones_row = hbf[0:1, 0:128]

    ktil = R6[:, 0:2048]
    kfm = R6[:, 2048:4096]
    vtm = cvv(R6, 4096, [16, 128])

    def mtile(j, kt):
        if j == 0:
            return mT_lo[:, kt, :]
        return mT_lo[:, 8 + kt, :] if kt < 8 else mT_hi[:, kt - 8, :]

    pbh = [nc.psum_tensor(f"pb{i}", [128, 512], F32) for i in range(6)]
    pb = [h.__enter__().ap() for h in pbh]
    pth = [nc.psum_tensor(f"pt{i}", [128, 512], BF16) for i in range(2)]
    pt = [h.__enter__().ap() for h in pth]

    dg = Ctr(nc.semaphore("dg").__enter__())
    ds = Ctr(nc.semaphore("ds").__enter__())
    mm = Ctr(nc.semaphore("mm").__enter__())
    ac = Ctr(nc.semaphore("ac").__enter__())
    vc = Ctr(nc.semaphore("vc").__enter__())
    so = Ctr(nc.semaphore("so").__enter__())

    EQ = {k: [] for k in "tvsgy"}

    def gdma(dst, src):
        n = dg.inc(16)
        EQ["g"].append(lambda e, d=dst, s=src: e.dma_start(out=d, in_=s).then_inc(dg.h, 16))
        return n

    def ydma(dst, src):
        n = ds.inc(16)
        EQ["y"].append(lambda e, d=dst, s=src: e.dma_start(out=d, in_=s).then_inc(ds.h, 16))
        return n

    def odma(dst, src):
        n = so.inc(16)
        EQ["y"].append(lambda e, d=dst, s=src: e.dma_start(out=d, in_=s).then_inc(so.h, 16))
        return n

    def wait(q, ctr, val):
        if val and val > 0:
            EQ[q].append(lambda e, hh=ctr.h, v=val: e.wait_ge(hh, v))

    def tap(nm, src_ap, ctr=None, val=None):
        if not dbg:
            return
        if ctr is not None:
            wait("g", ctr, val)
        gdma(dbg_out[nm], src_ap)

    bank_guard = {}

    def acq(bi):
        g = bank_guard.pop(bi, None)
        if g is not None:
            wait("t", g[0], g[1])

    def rel(bi, ctr, val):
        bank_guard[bi] = (ctr, val)

    def MM(bank, lhsT, rhs, start, stop, inc=False):
        if inc:
            mm.inc()
            EQ["t"].append(lambda e, b=bank, l=lhsT, r=rhs, s0=start, s1=stop:
                           e.matmul(b, l, r, start=s0, stop=s1).then_inc(mm.h, 1))
            return mm.n
        EQ["t"].append(lambda e, b=bank, l=lhsT, r=rhs, s0=start, s1=stop:
                       e.matmul(b, l, r, start=s0, stop=s1))
        return None

    def s_act(out, in_, func, bias=None, scale=None, accum=None):
        ac.inc()
        kw = {}
        if bias is not None: kw["bias"] = bias
        if scale is not None: kw["scale"] = scale
        if accum is not None: kw["accum_out"] = accum
        EQ["s"].append(lambda e, o=out, i=in_, f=func, kw=dict(kw):
                       e.activation(out=o, in_=i, func=f, **kw).then_inc(ac.h, 1))
        return ac.n

    def v_inc(fn):
        vc.inc()
        EQ["v"].append(lambda e, f=fn: f(e).then_inc(vc.h, 1))
        return vc.n

    def v_no(fn):
        EQ["v"].append(lambda e, f=fn: f(e))

    gdma(hbf, dr["csth"][:, :])
    gdma(identb, dr["cstf"][:, 0:128])
    ydma(identf, dr["cstf"][:, 0:128])
    n_cmisc = ydma(cmisc, dr["cstf"][:, 128:132])
    n_tauf = ydma(taust[:, 24:32], dr["taufloor"][:, :])
    gdma(wkidx_sb, dr["wk_idx"].rearrange("(dc p) n -> p dc n", p=128))
    gdma(wk_sb, dr["wk"].rearrange("(dc p) n -> p dc n", p=128))
    gdma(wv_sb, dr["wv"].rearrange("(dc p) n -> p dc n", p=128))
    n_xfk = [gdma(xfk[:, i * 4:(i + 1) * 4, :],
                  dr["xT"].rearrange("(dc p) t -> p dc t", p=128)[:, i * 4:(i + 1) * 4, :])
             for i in range(4)]

    wait("t", dg, n_xfk[3])

    def proj(outs, lhsT_fn, rhs_fn, nk, bis, n_out, engine="s"):
        for oi, osl in enumerate(outs):
            bi = bis[oi % len(bis)]
            acq(bi)
            bko = pb[bi][:, 0:n_out]
            for k in range(nk):
                MM(bko, lhsT_fn(oi, k), rhs_fn(oi, k), start=(k == 0), stop=(k == nk - 1),
                   inc=(k == nk - 1))
            if engine == "s":
                wait("s", mm, mm.n)
                rel(bi, ac, s_act(osl, bko, AF.Identity))
            else:
                wait("v", mm, mm.n)
                rel(bi, vc, v_inc(lambda e, o=osl, b=bko: e.tensor_copy(o, b)))

    proj([ktil[:, kg * 512:(kg + 1) * 512] for kg in range(4)],
         lambda oi, k: wkidx_sb[:, k, :], lambda oi, k: xfk[:, k, oi * 512:(oi + 1) * 512],
         16, [0, 1], 512)
    proj([kfm[:, kg * 512:(kg + 1) * 512] for kg in range(4)],
         lambda oi, k: wk_sb[:, k, :], lambda oi, k: xfk[:, k, oi * 512:(oi + 1) * 512],
         16, [2, 3], 512)
    proj([vtm[:, tt, :] for tt in range(16)],
         lambda oi, k: xfk[:, k, oi * 128:(oi + 1) * 128], lambda oi, k: wv_sb[:, k, :],
         16, [4, 5], 128)
    p1a_mm = mm.n

    wait("g", mm, p1a_mm)
    n_xfq = gdma(xfq[:, :, :], dr["xqT"].rearrange("(dc p) t -> p dc t", p=128))
    n_wiw = gdma(wiw_sb, dr["w_iw"].rearrange("(dc p) n -> p dc n", p=128))
    wait("t", dg, n_xfq)
    wait("t", dg, n_wiw)

    proj([iwsb[:, sub, :] for sub in range(8)],
         lambda oi, k: xfq[:, k, oi * 128:(oi + 1) * 128], lambda oi, k: wiw_sb[:, k, :],
         16, [0, 1], 16, engine="v")

    wq_prev = [0, 0]

    def stream_proj(outv, wname, bis):
        src = dr[wname].rearrange("(dc p) n -> p dc n", p=128)
        for cc in range(16):
            wait("g", mm, wq_prev[cc % 2])
            nld = gdma(wqb[cc % 2], src[:, :, cc * 128:(cc + 1) * 128])
            wait("t", dg, nld)
            for qh in range(2):
                bi = bis[(cc * 2 + qh) % len(bis)]
                acq(bi)
                for k in range(16):
                    MM(pb[bi], wqb[cc % 2][:, k, :], xfq[:, k, qh * 512:(qh + 1) * 512],
                       start=(k == 0), stop=(k == 15), inc=(k == 15))
                wq_prev[cc % 2] = mm.n
                wait("s", mm, mm.n)
                rel(bi, ac, s_act(outv[:, cc, qh * 512:(qh + 1) * 512], pb[bi], AF.Identity))

    stream_proj(qtil, "wq_idx", [2, 3])
    stream_proj(qfm, "wq", [4, 5])
    p1b_mm = mm.n
    tap("d_ktil", ktil, ac, ac.n)
    tap("d_kfm", kfm)
    tap("d_vtm", vtm[:, :, :].rearrange("p a b -> p (a b)") if False else R6[:, 4096:6144])
    tap("d_qtil", R1[:, 16384:32768])
    tap("d_qfm", R2[:, 0:16384])
    tap("d_iw", R5f.bitcast(F32)[:, 136:264] if False else R5[:, 272:528].bitcast(F32))

    wait("g", mm, p1b_mm)
    n_cau = gdma(caub[:, :, :], dr["causal"].rearrange("p (s n) -> p s n", s=4))
    wait("t", ac, ac.n)
    wait("v", dg, n_cau)
    wait("v", ds, n_tauf)
    wait("v", ds, n_cmisc)

    def ib_off(j, s4):
        return s4 * 1024 if j == 0 else 4096 + s4 * 2048

    oi = 0
    stt_use = {}
    for j in range(J):
        kgs = WJ[j] // 512
        for s4 in range(4):
            js = j * 4 + s4
            for kg in range(kgs):
                for h in range(HI):
                    bi = oi % 4
                    acq(bi)
                    MM(pb[bi], qtil[:, h, j * 512 + s4 * 128: j * 512 + s4 * 128 + 128],
                       ktil[:, kg * 512:(kg + 1) * 512], start=True, stop=True, inc=True)
                    rb = relub[oi % 2]
                    if oi >= 2:
                        wait("s", vc, stt_use[oi - 2])
                    wait("s", mm, mm.n)
                    na = s_act(rb, pb[bi], AF.Relu)
                    rel(bi, ac, na)
                    wait("v", ac, na)
                    isl = Iacc[:, kg * 512:(kg + 1) * 512]
                    iwc = iwsb[:, js, h:h + 1]
                    if h == 0:
                        if kg >= kgs - 2:
                            cau = caub[:, s4, (kg - (kgs - 2)) * 512:(kg - (kgs - 2) + 1) * 512]
                            nv = v_inc(lambda e, o=isl, r=rb, w=iwc, c=cau:
                                       e.scalar_tensor_tensor(out=o, in0=r, scalar=w, in1=c,
                                                              op0=ALU.mult, op1=ALU.add))
                        else:
                            nv = v_inc(lambda e, o=isl, r=rb, w=iwc:
                                       e.tensor_scalar(out=o, in0=r, scalar1=w, scalar2=None,
                                                       op0=ALU.mult))
                    else:
                        osl = isl if h < HI - 1 else Ibf[:, ib_off(j, s4) + kg * 512:
                                                        ib_off(j, s4) + (kg + 1) * 512]
                        nv = v_inc(lambda e, o=osl, r=rb, w=iwc, i=isl:
                                   e.scalar_tensor_tensor(out=o, in0=r, scalar=w, in1=i,
                                                          op0=ALU.mult, op1=ALU.add))
                    stt_use[oi] = nv
                    oi += 1
    p2_mm = mm.n

    def vrt(fn):
        n = v_inc(fn)
        wait("v", vc, n)
        return n

    lo = taust[:, 0:8]; cnt = taust[:, 8:16]; sel = taust[:, 16:24]
    tauf = taust[:, 24:32]; midc = taust[:, 32:40]
    vrt(lambda e: e.tensor_scalar(out=lo, in0=tauf, scalar1=0.0, scalar2=-1024.0,
                                  op0=ALU.mult, op1=ALU.add))
    step = 1024.0
    for it in range(NBIS):
        vrt(lambda e, st=step: e.tensor_scalar(out=midc, in0=lo, scalar1=st, scalar2=None,
                                               op0=ALU.add))
        for j in range(J):
            for s4 in range(4):
                js = j * 4 + s4
                isl = Ibf[:, ib_off(j, s4): ib_off(j, s4) + WJ[j]]
                v_no(lambda e, o=maskb[:, 0:WJ[j]], i=isl, m=midc[:, js:js + 1]:
                     e.tensor_scalar(out=o, in0=i, scalar1=m, scalar2=None, op0=ALU.is_ge))
                vrt(lambda e, o=cnt[:, js:js + 1], i=maskb[:, 0:WJ[j]]:
                    e.tensor_reduce(o, i, axis=AXX, op=ALU.add))
        vrt(lambda e: e.tensor_scalar(out=sel, in0=cnt, scalar1=float(TOPK), scalar2=None,
                                      op0=ALU.is_ge))
        vrt(lambda e, st=step: e.scalar_tensor_tensor(out=lo, in0=sel, scalar=st, in1=lo,
                                                      op0=ALU.mult, op1=ALU.add))
        step *= 0.5
    tau_done = vrt(lambda e: e.tensor_tensor(out=lo, in0=lo, in1=tauf, op=ALU.min))

    sqf = Iacc

    def emit_ln(xtile):
        vrt(lambda e, x=xtile: e.tensor_reduce(lns[:, 24:25], x, axis=AXX, op=ALU.add))
        vrt(lambda e: e.tensor_scalar(out=lns[:, 24:25], in0=lns[:, 24:25],
                                      scalar1=1.0 / 2048.0, scalar2=None, op0=ALU.mult))
        wait("s", vc, vc.n)
        nsq = s_act(sqf, xtile, AF.Square)
        wait("v", ac, nsq)
        vrt(lambda e: e.tensor_reduce(lns[:, 25:26], sqf, axis=AXX, op=ALU.add))
        vrt(lambda e: e.tensor_scalar(out=lns[:, 25:26], in0=lns[:, 25:26],
                                      scalar1=1.0 / 2048.0, scalar2=None, op0=ALU.mult))
        vrt(lambda e: e.tensor_tensor(out=lns[:, 26:27], in0=lns[:, 24:25],
                                      in1=lns[:, 24:25], op=ALU.mult))
        nvar = vrt(lambda e: e.tensor_tensor(out=lns[:, 25:26], in0=lns[:, 25:26],
                                             in1=lns[:, 26:27], op=ALU.subtract))
        wait("s", vc, nvar)
        nsd = s_act(lns[:, 27:28], lns[:, 25:26], AF.Sqrt, bias=EPS_AP)
        wait("v", ac, nsd)
        vrt(lambda e: e.reciprocal(lns[:, 28:29], lns[:, 27:28]))
        return v_inc(lambda e, x=xtile: e.tensor_scalar(out=x, in0=x, scalar1=lns[:, 24:25],
                                                        scalar2=lns[:, 28:29],
                                                        op0=ALU.subtract, op1=ALU.mult))

    wait("t", vc, tau_done)
    toi = 0
    tr_evac = {}
    prev_mask_tr = 0
    for j in range(J):
        for s4 in range(4):
            js = j * 4 + s4
            isl = Ibf[:, ib_off(j, s4): ib_off(j, s4) + WJ[j]]
            wait("v", mm, prev_mask_tr)
            mrd = v_inc(lambda e, o=maskb[:, 0:WJ[j]], i=isl, tc=lo[:, js:js + 1]:
                        e.tensor_scalar(out=o, in0=i, scalar1=tc, scalar2=None, op0=ALU.is_ge))
            for kt in range(KTJ[j]):
                bk = pt[toi % 2]
                if toi >= 2:
                    wait("t", vc, tr_evac[toi - 2])
                wait("t", vc, mrd)
                mm.inc()
                EQ["t"].append(lambda e, b=bk[:, 0:128], i=maskb[:, kt * 128:(kt + 1) * 128]:
                               e.transpose(b, i, identb).then_inc(mm.h, 1))
                prev_mask_tr = mm.n
                wait("v", mm, mm.n)
                tr_evac[toi] = v_inc(lambda e, o=mtile(j, kt)[:, s4 * 128:(s4 + 1) * 128],
                                     b=bk[:, 0:128]: e.tensor_copy(o, b))
                toi += 1
    tap("d_Ibf", Ibf, vc, tau_done)
    tap("d_tau", R5[:, 528:624].bitcast(F32))
    tap("d_mT", R3[:, 12288:20480], vc, vc.n)
    tap("d_mThi", R3[:, 0:4096])
    p3_mm = mm.n
    p3_vc = vc.n
    p3_ac = ac.n

    for h_ in reversed(pth):
        h_.__exit__(None, None, None)
    pb += [h.__enter__().ap() for h in
           [nc.psum_tensor("pb6", [128, 512], F32), nc.psum_tensor("pb7", [128, 512], F32)]]

    wait("y", mm, p2_mm)
    n_xq = [ydma(xqf[:, i * 2:(i + 1) * 2, :],
                 dr["xq"].rearrange("(tt p) d -> p tt d", p=128)[:, i * 2:(i + 1) * 2, :])
            for i in range(4)]
    wait("s", ds, n_cmisc)
    wait("t", ac, p3_ac)
    wait("t", vc, p3_vc)
    bank_guard.clear()

    last_pbuf = [0, 0]
    prev_bc_mm = 0
    gi = 0
    for j in range(J):
        for h in range(H):
            gpar = gi % 2
            sb0, sb1 = (0, 1) if gpar == 0 else (4, 5)
            pvb = 2 if gpar == 0 else 3
            dnb = 6 if gpar == 0 else 7
            acq(pvb)
            acq(dnb)
            kts = KTJ[j]
            for kt in range(kts):
                bi = (sb0, sb1)[kt % 2]
                acq(bi)
                MM(pb[bi], kfm[:, kt * 128:(kt + 1) * 128], qfm[:, h, j * 512:(j + 1) * 512],
                   start=True, stop=True, inc=True)
                sc_mm = mm.n
                pbp = pbuf[kt % 2]
                wait("s", mm, last_pbuf[kt % 2])
                wait("s", mm, sc_mm)
                na = s_act(pbp, pb[bi], AF.Exp, bias=EXPB_AP, scale=SCL)
                rel(bi, ac, na)
                wait("v", ac, na)
                nv = v_inc(lambda e, o=pbp, m=mtile(j, kt):
                           e.tensor_tensor(out=o, in0=o, in1=m, op=ALU.mult))
                wait("t", vc, nv)
                MM(pb[pvb], vtm[:, kt, :], pbp, start=(kt == 0), stop=(kt == kts - 1),
                   inc=False)
                last_pbuf[kt % 2] = MM(pb[dnb][0:1, 0:512], ones_col, pbp,
                                       start=(kt == 0), stop=(kt == kts - 1), inc=True)
            grp_mm = mm.n
            wait("v", mm, grp_mm)
            wait("v", mm, prev_bc_mm)
            nden = v_inc(lambda e, d=pb[dnb][0:1, 0:512]: e.reciprocal(rrow[0:1, 0:512], d))
            rel(dnb, vc, nden)
            acq(sb0)
            wait("t", vc, nden)
            MM(pb[sb0], ones_row, rrow[0:1, 0:512], start=True, stop=True, inc=True)
            prev_bc_mm = mm.n
            wait("v", mm, prev_bc_mm)
            nbc = v_inc(lambda e, b=pb[sb0]: e.tensor_copy(bcast_sb, b))
            rel(sb0, vc, nbc)
            osl = ofm[:, h, j * 512:(j + 1) * 512]
            nev = v_inc(lambda e, o=osl, b=pb[pvb]:
                        e.scalar_tensor_tensor(out=o, in0=b, scalar=1.0, in1=bcast_sb,
                                               op0=ALU.mult, op1=ALU.mult))
            rel(pvb, vc, nev)
            gi += 1
    tap("d_ofm", R2[:, 16384:32768], vc, vc.n)
    p5_mm = mm.n
    p5_vc = vc.n
    p5_ac = ac.n

    for i in range(4):
        wait("v", ds, n_xq[i])
    wait("g", vc, p5_vc)
    wait("t", ac, p5_ac)
    wait("t", vc, p5_vc)
    bank_guard.clear()

    wo_prev = [0, 0]
    for ttg in range(4):
        for hc in range(16):
            wait("g", mm, wo_prev[hc % 2])
            nld = gdma(wob[hc % 2], dr["wo"][hc * 128:(hc + 1) * 128, :])
            wait("t", dg, nld)
            if hc == 0:
                for bi in range(8):
                    acq(bi)
            for tt2 in range(2):
                for dc in range(4):
                    bi = tt2 * 4 + dc
                    MM(pb[bi], ofm[:, hc, ttg * 256 + tt2 * 128: ttg * 256 + tt2 * 128 + 128],
                       wob[hc % 2][:, dc * 512:(dc + 1) * 512],
                       start=(hc == 0), stop=(hc == 15), inc=(bi == 7))
            wo_prev[hc % 2] = mm.n
        grp_mm = mm.n
        wait("v", mm, grp_mm)
        for tt2 in range(2):
            tt = ttg * 2 + tt2
            for dc in range(4):
                bi = tt2 * 4 + dc
                xsl = xqf[:, tt, dc * 512:(dc + 1) * 512]
                rel(bi, vc, v_inc(lambda e, o=xsl, b=pb[bi]:
                                  e.scalar_tensor_tensor(out=o, in0=b, scalar=1.0, in1=o,
                                                         op0=ALU.mult, op1=ALU.add)))
            nnorm = emit_ln(xqf[:, tt, :])
            wait("y", vc, nnorm)
            n_spill = ydma(spill_d[tt * 128:(tt + 1) * 128, :], xqf[:, tt, :])
            for dcq in range(16):
                bti = tt2 * 4 + (dcq % 2)
                acq(bti)
                wait("t", vc, nnorm)
                mm.inc()
                EQ["t"].append(lambda e, b=pb[bti][:, 0:128],
                               i=xqf[:, tt, dcq * 128:(dcq + 1) * 128]:
                               e.transpose(b, i, identf).then_inc(mm.h, 1))
                wait("v", mm, mm.n)
                rel(bti, vc, v_inc(lambda e, o=xnfm[:, dcq, tt * 128:(tt + 1) * 128],
                                   b=pb[bti][:, 0:128]: e.tensor_copy(o, b)))
    tap("d_xnfm", R3[:, 0:16384], vc, vc.n)
    p6_mm = mm.n
    p6_vc = vc.n
    p6_ac = ac.n
    spill_all = ds.n
    if dbg:
        wait("g", ds, spill_all)
        gdma(dbg_out["d_xn"], spill_d[:, :])

    wait("v", ds, spill_all)
    wait("v", mm, p6_mm)
    wait("g", mm, p6_mm)
    wait("t", ac, p6_ac)
    w1_prev = [0, 0]
    w2_prev = [0, 0]
    for tch in range(2):
        if tch > 0:
            wait("t", vc, vc.n)
            wait("t", ac, ac.n)
            bank_guard.clear()
        for hc in range(64):
            wait("g", mm, w1_prev[hc % 2])
            gdma(w1b[hc % 2], dr["w1t"].rearrange("(dc p) n -> p dc n", p=128)
                 [:, :, hc * 128:(hc + 1) * 128])
            n3 = gdma(w3b[hc % 2], dr["w3t"].rearrange("(dc p) n -> p dc n", p=128)
                      [:, :, hc * 128:(hc + 1) * 128])
            wait("t", dg, n3)
            bU, bL, bH = hc % 2, 2 + hc % 2, 4 + hc % 2
            acq(bU); acq(bL); acq(bH)
            for k in range(16):
                MM(pb[bU], w1b[hc % 2][:, k, :], xnfm[:, k, tch * 512:(tch + 1) * 512],
                   start=(k == 0), stop=(k == 15), inc=(k == 15))
            up_mm = mm.n
            for k in range(16):
                MM(pb[bL], w3b[hc % 2][:, k, :], xnfm[:, k, tch * 512:(tch + 1) * 512],
                   start=(k == 0), stop=(k == 15), inc=(k == 15))
            lin_mm = mm.n
            w1_prev[hc % 2] = lin_mm
            wait("s", mm, up_mm)
            nsil = s_act(silub[hc % 2], pb[bU], AF.Silu)
            rel(bU, ac, nsil)
            wait("v", mm, lin_mm)
            wait("v", ac, nsil)
            hsl = hid[:, hc, :]
            nstt = v_inc(lambda e, o=hsl, b=pb[bL], sb=silub[hc % 2]:
                         e.scalar_tensor_tensor(out=o, in0=b, scalar=FWS, in1=sb,
                                                op0=ALU.mult, op1=ALU.mult))
            rel(bL, vc, nstt)
            wait("t", vc, nstt)
            MM(pb[bH], hbf, hsl, start=True, stop=True, inc=True)
            wait("v", mm, mm.n)
            rel(bH, vc, v_inc(lambda e, o=hsl, b=pb[bH]: e.tensor_copy(o, b)))
        bf_done = {}
        for tsl in range(4):
            last = None
            src_is_hid = True
            for st in range(6):
                hstep = 1 << st
                if src_is_hid:
                    s3 = hid[:, :, tsl * 128:(tsl + 1) * 128]
                    d3 = htmp[:, :, :]
                else:
                    s3 = htmp[:, :, :]
                    d3 = hid[:, :, tsl * 128:(tsl + 1) * 128]
                s5 = s3.rearrange("p (b two g) t -> p b two g t", two=2, g=hstep)
                d5 = d3.rearrange("p (b two g) t -> p b two g t", two=2, g=hstep)
                v_no(lambda e, a=s5[:, :, 0:1, :, :], b2=s5[:, :, 1:2, :, :],
                     o=d5[:, :, 0:1, :, :]: e.tensor_tensor(out=o, in0=a, in1=b2, op=ALU.add))
                last = v_inc(lambda e, a=s5[:, :, 0:1, :, :], b2=s5[:, :, 1:2, :, :],
                             o=d5[:, :, 1:2, :, :]:
                             e.tensor_tensor(out=o, in0=a, in1=b2, op=ALU.subtract))
                src_is_hid = not src_is_hid
            bf_done[tsl] = last
        if dbg and tch == 0:
            tap("d_hid", R1[:, 0:32768], vc, vc.n)
        for sub in range(2):
            wait("t", vc, bf_done[sub * 2 + 1])
            wait("t", ac, ac.n)
            for hc in range(64):
                wait("g", mm, w2_prev[hc % 2])
                nld = gdma(w2b[hc % 2], dr["w2t"][hc * 128:(hc + 1) * 128, :])
                wait("t", dg, nld)
                if hc == 0:
                    for bi in range(8):
                        acq(bi)
                for tt2 in range(2):
                    tsl = sub * 2 + tt2
                    for dc in range(4):
                        bi = tt2 * 4 + dc
                        MM(pb[bi], hid[:, hc, tsl * 128:(tsl + 1) * 128],
                           w2b[hc % 2][:, dc * 512:(dc + 1) * 512],
                           start=(hc == 0), stop=(hc == 63), inc=(bi == 7))
                w2_prev[hc % 2] = mm.n
            grp_mm = mm.n
            for tt2 in range(2):
                tti = tch * 4 + sub * 2 + tt2
                xb = xnr[tt2]
                nrl = ydma(xb, spill_d[tti * 128:(tti + 1) * 128, :])
                wait("v", ds, nrl)
                wait("v", mm, grp_mm)
                for dc in range(4):
                    bi = tt2 * 4 + dc
                    xsl = xb[:, dc * 512:(dc + 1) * 512]
                    rel(bi, vc, v_inc(lambda e, o=xsl, b=pb[bi]:
                                      e.scalar_tensor_tensor(out=o, in0=b, scalar=1.0,
                                                             in1=o, op0=ALU.mult,
                                                             op1=ALU.add)))
                nnorm = emit_ln(xb)
                wait("y", vc, nnorm)
                odma(out_d[tti * 128:(tti + 1) * 128, :], xb)
                wait("y", so, so.n)

    wait("y", so, so.n)

    lp = nc.allow_low_precision(reason="bf16 elementwise by design; fp32 accumulation kept")
    lp.__enter__()
    with nc.Block() as block:
        @block.tensor
        def _(e):
            for f in EQ["t"]:
                f(e)

        @block.vector
        def _(e):
            for f in EQ["v"]:
                f(e)

        @block.scalar
        def _(e):
            for f in EQ["s"]:
                f(e)

        @block.gpsimd
        def _(e):
            for f in EQ["g"]:
                f(e)

        @block.sync
        def _(e):
            for f in EQ["y"]:
                f(e)

    lp.__exit__(None, None, None)
    return nc


def _host_constants():
    Hm = np.array([[1.0]], dtype=np.float32)
    n = 1
    while n < 128:
        Hm = np.block([[Hm, Hm], [Hm, -Hm]]).astype(np.float32)
        n *= 2
    cstf = np.zeros((128, 132), dtype=np.float32)
    cstf[:, 0:128] = np.eye(128, dtype=np.float32)
    cstf[:, 128] = EXPB
    cstf[:, 129] = 1e-5
    cstf[:, 130] = 0.0
    cstf[:, 131] = -1024.0
    return Hm, cstf


def kernel(L0_attn_x, L0_attn_wq_idx, L0_attn_wk_idx, L0_attn_w_iw, L0_attn_wq,
           L0_attn_wk, L0_attn_wv, L0_attn_wo, L0_ffn_W1, L0_ffn_W2, L0_ffn_W3):
    x = np.ascontiguousarray(np.asarray(L0_attn_x, dtype=np.float32))
    Hm, cstf = _host_constants()
    shared = {
        "cstf": cstf, "csth": Hm,
        "wq_idx": np.ascontiguousarray(np.asarray(L0_attn_wq_idx, dtype=np.float32)),
        "wk_idx": np.ascontiguousarray(np.asarray(L0_attn_wk_idx, dtype=np.float32)),
        "w_iw": np.ascontiguousarray(np.asarray(L0_attn_w_iw, dtype=np.float32)),
        "wq": np.ascontiguousarray(np.asarray(L0_attn_wq, dtype=np.float32)),
        "wk": np.ascontiguousarray(np.asarray(L0_attn_wk, dtype=np.float32)),
        "wv": np.ascontiguousarray(np.asarray(L0_attn_wv, dtype=np.float32)),
        "wo": np.ascontiguousarray(np.asarray(L0_attn_wo, dtype=np.float32)),
        "w1t": np.ascontiguousarray(np.asarray(L0_ffn_W1, dtype=np.float32).T),
        "w3t": np.ascontiguousarray(np.asarray(L0_ffn_W3, dtype=np.float32).T),
        "w2t": np.ascontiguousarray(np.asarray(L0_ffn_W2, dtype=np.float32).T),
    }
    tl = np.arange(128)[:, None]
    in_maps = []
    rows_all = []
    for c in range(8):
        b, par = c // 2, c % 2
        rows = np.concatenate([np.arange(1024 * j + 512 * par, 1024 * j + 512 * par + 512)
                               for j in range(J)])
        rows_all.append((b, rows))
        xb = x[b]
        xq = np.ascontiguousarray(xb[rows])
        causal = np.zeros((128, 4096), dtype=np.float32)
        for s4 in range(4):
            kk = np.arange(1024)[None, :]
            ok = kk <= 512 * par + 128 * s4 + tl
            causal[:, s4 * 1024:(s4 + 1) * 1024] = np.where(ok, 0.0, SENT)
        taufl = np.zeros((128, 8), dtype=np.float32)
        for j in range(J):
            for s4 in range(4):
                gt = 1024 * j + 512 * par + 128 * s4 + np.arange(128)
                taufl[:, j * 4 + s4] = np.where(gt < TOPK, -30000.0, 1e9)
        m = dict(shared)
        m["xT"] = np.ascontiguousarray(xb.T)
        m["xqT"] = np.ascontiguousarray(xq.T)
        m["xq"] = xq
        m["causal"] = causal
        m["taufloor"] = taufl
        in_maps.append(m)

    nc = build_nc()
    res = run_bass_kernel_spmd(nc, in_maps, core_ids=list(range(8)))
    y = np.empty((B, S, D), dtype=np.float32)
    for c in range(8):
        b, rows = rows_all[c]
        y[b][rows] = res.results[c]["out"]
    return y
